# version 13
# baseline (speedup 1.0000x reference)
import contextlib
import numpy as np
import concourse.bass as bass
import concourse.mybir as mybir
from concourse.bass_utils import run_bass_kernel_spmd



F32 = mybir.dt.float32
BF16 = mybir.dt.bfloat16
I32 = mybir.dt.int32
AF = mybir.ActivationFunctionType
ALU = mybir.AluOpType
AX = mybir.AxisListType

D = 2048
NT = 16
EH = 1408
NE = 8
NBANK = 4
QH, KVH, HD, WIN = 16, 4, 128, 512
NSLOT = 16
KTOP = 4

ENGINES = ("tensor", "vector", "scalar", "gpsimd", "sync")
SEM_STEP = {"tensor": 1, "vector": 1, "scalar": 1, "gpsimd_c": 1,
            "gpsimd_d": 16, "sync_d": 16}




F32 = mybir.dt.float32
BF16 = mybir.dt.bfloat16
I32 = mybir.dt.int32
AF = mybir.ActivationFunctionType
ALU = mybir.AluOpType
AX = mybir.AxisListType

D = 2048
NT = 16
EH = 1408
NE = 8
NBANK = 4
QH, KVH, HD, WIN = 16, 4, 128, 512
NSLOT = 16
KTOP = 4

ENGINES = ("tensor", "vector", "scalar", "gpsimd", "sync")
SEM_STEP = {"tensor": 1, "vector": 1, "scalar": 1, "gpsimd_c": 1,
            "gpsimd_d": 16, "sync_d": 16}


class Op:
    __slots__ = ("engine", "fn", "reads", "writes", "sem_eng", "deps",
                 "milestoned", "mi", "drain_before", "idx")

    def __init__(self, engine, fn, reads, writes, sem_eng):
        self.engine = engine
        self.fn = fn
        self.reads = reads
        self.writes = writes
        self.sem_eng = sem_eng
        self.deps = set()
        self.milestoned = False
        self.mi = None
        self.drain_before = False
        self.idx = None


class Sched:
    def __init__(self, nc):
        self.nc = nc
        self.ops = {e: [] for e in ENGINES}
        self.all_ops = []
        self.last_writer = {}
        self.readers = {}
        self._st = contextlib.ExitStack()
        self._sems = {}
        self._tensors = {}

    def sbuf(self, name, shape, dtype):
        t = self._st.enter_context(self.nc.sbuf_tensor(name, list(shape), dtype))
        self._tensors[name] = t
        return t

    def psum(self, name, shape, dtype=F32):
        t = self._st.enter_context(self.nc.psum_tensor(name, list(shape), dtype))
        self._tensors[name] = t
        return t

    def sem(self, name):
        if name not in self._sems:
            self._sems[name] = self._st.enter_context(self.nc.semaphore(name))
        return self._sems[name]

    def reg(self, eng, name):
        return self._st.enter_context(getattr(self.nc, eng).register(name))

    def op(self, engine, fn, reads=(), writes=(), dma=False):
        sem_eng = engine
        if engine == "gpsimd":
            sem_eng = "gpsimd_d" if dma else "gpsimd_c"
        elif engine == "sync":
            sem_eng = "sync_d"
        o = Op(engine, fn, list(reads), list(writes), sem_eng)
        for k in o.reads:
            w = self.last_writer.get(k)
            if w is not None:
                o.deps.add(w)
        for k in o.writes:
            w = self.last_writer.get(k)
            if w is not None:
                o.deps.add(w)
            for r in self.readers.get(k, ()):
                o.deps.add(r)
        for k in o.writes:
            self.last_writer[k] = o
            self.readers[k] = []
        for k in o.reads:
            self.readers.setdefault(k, []).append(o)
        o.idx = len(self.ops[engine])
        self.ops[engine].append(o)
        self.all_ops.append(o)
        return o

    def dma(self, queue, out_ap, in_ap, reads=(), writes=(), **kw):
        assert queue in ("gpsimd", "sync")
        return self.op(queue,
                       lambda h, o=out_ap, i=in_ap: h.dma_start(out=o, in_=i, **kw),
                       reads=reads, writes=writes, dma=True)

    def finalize(self):
        for o in self.all_ops:
            pass
        for o in self.all_ops:
            if o.sem_eng in ("gpsimd_d", "sync_d"):
                o.milestoned = True
            for d in o.deps:
                if d.sem_eng != o.sem_eng or d.sem_eng in ("gpsimd_d", "sync_d"):
                    d.milestoned = True
                elif d.engine == o.engine and o.engine in ("vector", "scalar", "gpsimd"):
                    o.drain_before = True
        counters = {k: 0 for k in SEM_STEP}
        for e in ENGINES:
            for o in self.ops[e]:
                if o.milestoned:
                    counters[o.sem_eng] += 1
                    o.mi = counters[o.sem_eng] * SEM_STEP[o.sem_eng]
        self._batch_prev_end = {}
        for e in ("gpsimd", "sync"):
            run = []
            prev_end = [0]
            def close(run, prev_end):
                if not run:
                    return
                last_mi = max((o.mi or 0) for o in run)
                for o in run:
                    if o.mi is not None:
                        o.mi = last_mi
                if last_mi:
                    prev_end[0] = last_mi
            for o in self.ops[e]:
                is_dma = o.sem_eng in ("gpsimd_d", "sync_d")
                has_wait = any(d.mi is not None and not (
                    d.sem_eng == o.sem_eng and d.sem_eng not in ("gpsimd_d", "sync_d"))
                    for d in o.deps)
                if not is_dma or has_wait:
                    close(run, prev_end)
                    run = []
                if is_dma:
                    if not run:
                        self._batch_prev_end[id(o)] = prev_end[0]
                    run.append(o)
            close(run, prev_end)
        sems = {k: self.sem("S_" + k) for k, c in counters.items() if c > 0}
        waited = {e: {k: 0 for k in SEM_STEP} for e in ENGINES}

        def make_body(e):
            ops = self.ops[e]

            def body(h):
                w = waited[e]
                for o in ops:
                    for d in sorted(o.deps, key=lambda x: (x.sem_eng, x.mi or 0)):
                        if d.sem_eng == o.sem_eng and d.sem_eng not in ("gpsimd_d", "sync_d"):
                            continue
                        if d.mi is None:
                            continue
                        if d.mi > w[d.sem_eng]:
                            h.wait_ge(sems[d.sem_eng], d.mi)
                            w[d.sem_eng] = d.mi
                    pe = self._batch_prev_end.get(id(o))
                    if pe and pe > w[o.sem_eng]:
                        h.wait_ge(sems[o.sem_eng], pe)
                        w[o.sem_eng] = pe
                    if o.drain_before:
                        h.drain()
                    inst = o.fn(h)
                    if o.milestoned:
                        assert inst is not None, f"milestoned op on {e} returned no instruction"
                        inst.then_inc(sems[o.sem_eng], SEM_STEP[o.sem_eng])
            return body

        with self._st:
            with self.nc.Block() as block:
                for e in ENGINES:
                    if self.ops[e]:
                        getattr(block, e)(make_body(e))
        return self.nc



NEG = -1e30


class C:
    pass


def alloc_common(S, TOK, gcols=0, wcols=4096, need_r0=True):
    c = C()
    c.TOK = TOK
    c.NC = TOK // 512 if TOK >= 512 else 1
    c.CW = min(TOK, 512)
    c.x = S.sbuf("x_res", [128, NT * TOK], BF16)
    c.A = S.sbuf("bufA", [128, NT * TOK], BF16)
    c.B = S.sbuf("bufB", [128, NT * TOK], BF16)
    c.wcols = wcols
    c.W = (S.sbuf("w_buf0", [128, wcols], BF16), S.sbuf("w_buf1", [128, wcols], BF16))
    if gcols:
        c.G = S.sbuf("bufG", [128, gcols], BF16)
    c.pW0 = S.psum("pW0", [128, 1024], F32)
    c.pW1 = S.psum("pW1", [128, 1024], F32)
    c.pA = S.psum("pA", [128, 512], F32)
    c.pB = S.psum("pB", [128, 512], F32)
    c.pC = S.psum("pC", [128, 512], F32)
    c.pBF = S.psum("pBF", [128, 1024], BF16)
    c.idn = S.sbuf("sb_idn_f32", [128, 128], F32)
    c.idnb = S.sbuf("sb_idn_bf", [128, 128], BF16)
    c.onesb = S.sbuf("sb_ones_bf", [1, 128], BF16)
    c.onesf = S.sbuf("sb_ones_f32", [1, 128], F32)
    c.invdcol = S.sbuf("sb_invd_bf", [128, 2], BF16)
    c.epscol = S.sbuf("sb_eps_col", [128, 2], F32)
    if need_r0:
        c.r0 = S.sbuf("row0", [1, TOK], F32)
    c.r1 = S.sbuf("row1", [1, TOK], F32)
    c.r2 = S.sbuf("row2", [1, TOK], F32)
    c.sqb = (S.sbuf("sq0", [128, c.CW], BF16), S.sbuf("sq1", [128, c.CW], BF16))
    c.t32 = (S.sbuf("t32a", [128, c.CW], F32), S.sbuf("t32b", [128, c.CW], F32))
    c.colsc = S.sbuf("colsc", [128, 16], F32)
    return c


def load_consts(S, c, IN):
    for t, nm in [(c.idn, "idn_f32"), (c.idnb, "idn_bf"), (c.onesb, "ones_bf"),
                  (c.onesf, "ones_f32"), (c.invdcol, "invd_bf"), (c.epscol, "eps_col")]:
        S.dma("gpsimd", t[:, :], IN[nm][:, :], writes=[t.name])


def load_x(S, c, xin, key="x"):
    for t in range(NT):
        S.dma("gpsimd", c.x[:, t * c.TOK:(t + 1) * c.TOK],
              xin[t * 128:(t + 1) * 128, :], writes=[f"{key}{t}"])


def store_x(S, c, xout, key="x", tag="xout"):
    outs = []
    for t in range(NT):
        S.dma("gpsimd", xout[t * 128:(t + 1) * 128, :],
              c.x[:, t * c.TOK:(t + 1) * c.TOK],
              reads=[f"{key}{t}"], writes=[f"{tag}{t}"])
        outs.append(f"{tag}{t}")
    return outs


def emit_norm_stats(S, c, kind, eps_idx, xkeys=None, dst_rstd=None):
    TOK, CW, NCh = c.TOK, c.CW, c.NC
    xkeys = xkeys or [f"x{t}" for t in range(NT)]
    if kind == "ln":
        for ch in range(NCh):
            for t in range(NT):
                xsl = c.x[:, t * TOK + ch * CW: t * TOK + (ch + 1) * CW]
                S.op("tensor", lambda h, ch=ch, t=t, xsl=xsl:
                     h.matmul(c.pW0[0:1, ch * CW:(ch + 1) * CW], c.invdcol[:, 0:1],
                              xsl, start=(t == 0), stop=(t == NT - 1)),
                     reads=[xkeys[t], c.invdcol.name], writes=["pW0"])
    for ch in range(NCh):
        for t in range(NT):
            xsl = c.x[:, t * TOK + ch * CW: t * TOK + (ch + 1) * CW]
            sq = c.sqb[t % 2]
            S.op("vector", lambda h, o=sq, i=xsl: h.tensor_mul(o[:, :], i, i),
                 reads=[xkeys[t]], writes=[sq.name])
            S.op("tensor", lambda h, ch=ch, t=t, sq=sq:
                 h.matmul(c.pW1[0:1, ch * CW:(ch + 1) * CW], c.invdcol[:, 0:1],
                          sq[:, :], start=(t == 0), stop=(t == NT - 1)),
                 reads=[sq.name, c.invdcol.name], writes=["pW1"])
    if kind == "ln":
        S.op("vector", lambda h: h.tensor_copy(c.r0[:1, :], c.pW0[0:1, 0:TOK]),
             reads=["pW0"], writes=[c.r0.name])
        S.op("vector", lambda h: h.tensor_mul(c.r2[:1, :], c.r0[:1, :], c.r0[:1, :]),
             reads=[c.r0.name], writes=[c.r2.name])
        S.op("vector", lambda h: h.tensor_sub(c.r1[:1, :], c.pW1[0:1, 0:TOK], c.r2[:1, :]),
             reads=["pW1", c.r2.name], writes=[c.r1.name])
    else:
        S.op("vector", lambda h: h.tensor_copy(c.r1[:1, :], c.pW1[0:1, 0:TOK]),
             reads=["pW1"], writes=[c.r1.name])
    S.op("scalar", lambda h: h.activation(c.r2[:1, :], c.r1[:1, :], AF.Ln,
                                          bias=c.epscol[0:1, eps_idx:eps_idx + 1]),
         reads=[c.r1.name, c.epscol.name], writes=[c.r2.name])
    S.op("scalar", lambda h: h.activation(c.r1[:1, :], c.r2[:1, :], AF.Exp, scale=-0.5),
         reads=[c.r2.name], writes=[c.r1.name])


def emit_bcast_row(S, c, row_ap, psum, psum_key, row_key, width, f32=True):
    ones = c.onesf if f32 else c.onesb
    nch = (width + 511) // 512
    for ch in range(nch):
        w = min(512, width - ch * 512)
        S.op("tensor", lambda h, ch=ch, w=w, row_ap=row_ap:
             h.matmul(psum[:, ch * 512: ch * 512 + w], ones[0:1, :],
                      row_ap[0:1, ch * 512: ch * 512 + w], start=True, stop=True),
             reads=[row_key, ones.name], writes=[psum_key])


def emit_layernorm(S, c, dst, eps_idx=0, also_x=False, dstname=None,
                   xkeys=None, dkeys=None):
    TOK = c.TOK
    xkeys = xkeys or [f"x{t}" for t in range(NT)]
    dkeys = dkeys or [f"{dstname or dst.name}{t}" for t in range(NT)]
    emit_norm_stats(S, c, "ln", eps_idx, xkeys=xkeys)
    emit_bcast_row(S, c, c.r0, c.pW0, "pW0", c.r0.name, TOK)
    emit_bcast_row(S, c, c.r1, c.pW1, "pW1", c.r1.name, TOK)
    for t in range(NT):
        xt = c.x[:, t * TOK:(t + 1) * TOK]
        dt_ = dst[:, t * TOK:(t + 1) * TOK]
        tb = c.t32[t % 2]
        for ch in range(c.NC):
            sl = slice(ch * c.CW, (ch + 1) * c.CW)
            S.op("vector", lambda h, xt=xt, sl=sl, tb=tb:
                 h.tensor_sub(tb[:, :], xt[:, sl], c.pW0[:, sl]),
                 reads=[xkeys[t], "pW0"], writes=[tb.name])
            S.op("vector", lambda h, dt_=dt_, sl=sl, tb=tb:
                 h.tensor_mul(dt_[:, sl], tb[:, :], c.pW1[:, sl]),
                 reads=[tb.name, "pW1"], writes=[dkeys[t]])
            if also_x:
                S.op("vector", lambda h, xt=xt, sl=sl, tb=tb:
                     h.tensor_mul(xt[:, sl], tb[:, :], c.pW1[:, sl]),
                     reads=[tb.name, "pW1"], writes=[xkeys[t]])


def emit_moe(S, c, IN, call_id):
    TOK, CW, NCh = c.TOK, c.CW, c.NC
    emit_layernorm(S, c, c.A, eps_idx=0, dstname="A")
    if not hasattr(c, "_moe"):
        m = c._moe = C()
        m.gwTf = S.sbuf("moe_gwTf", [128, NT * 8], F32)
        m.gwT = S.sbuf("moe_gwT", [128, NT * 8], BF16)
        m.gb2 = S.sbuf("moe_gb2", [8, 2], F32)
        m.gb = S.sbuf("moe_gb", [8, 1], F32)
        m.b2sb = S.sbuf("moe_b2sb", [8, D], BF16)
        m.b1c = S.sbuf("moe_b1c", [128, 11], F32)
        m.scores = S.sbuf("moe_scores", [8, TOK], F32)
        m.ctok = S.sbuf("moe_ctok", [128, 8], BF16)
        m.combb = S.sbuf("moe_combb", [8, TOK], BF16)
        m.cbc = S.sbuf("moe_cbc", [128, TOK], F32)
        m.tm1 = S.sbuf("moe_tm1", [128, 8], F32)
        m.tm2 = S.sbuf("moe_tm2", [128, 8], F32)
        m.tl2 = S.sbuf("moe_tl2", [128, 8], F32)
        m.hs = (S.sbuf("moe_hs0", [128, CW], F32), S.sbuf("moe_hs1", [128, CW], F32))
        m.crow = S.sbuf("moe_crow", [1, TOK], BF16)
    m = c._moe
    gwT, gb2, gb, b2sb, b1c = m.gwT, m.gb2, m.gb, m.b2sb, m.b1c
    scores, ctok, combb, cbc = m.scores, m.ctok, m.combb, m.cbc
    tm1, tm2, tl2, hs, crow = m.tm1, m.tm2, m.tl2, m.hs, m.crow
    S.dma("gpsimd", m.gwTf[:, :],
          bass.AP(IN["gwT_h"], 0, [[8, 128], [128 * 8, NT], [1, 8]]),
          writes=[m.gwTf.name])
    S.op("scalar", lambda h: h.copy(gwT[:, :], m.gwTf[:, :]),
         reads=[m.gwTf.name], writes=[gwT.name])
    S.dma("gpsimd", gb2[:, :], IN["gb2"][:, :], writes=[gb2.name])
    S.op("vector", lambda h: h.tensor_add(gb[:, 0:1], gb2[:, 0:1], gb2[:, 1:2]),
         reads=[gb2.name], writes=[gb.name])
    S.dma("gpsimd", b2sb[:, :], IN["b2"][:, :], writes=[b2sb.name])
    hfull = c.G

    psc = (c.pA, c.pB)
    for ch in range(NCh):
        p = psc[ch % 2]
        pk = "pA" if ch % 2 == 0 else "pB"
        for t in range(NT):
            S.op("tensor", lambda h, p=p, ch=ch, t=t:
                 h.matmul(p[0:8, 0:CW], gwT[:, t * 8:(t + 1) * 8],
                          c.A[:, t * TOK + ch * CW: t * TOK + (ch + 1) * CW],
                          start=(t == 0), stop=(t == NT - 1)),
                 reads=[gwT.name, f"A{t}"], writes=[pk])
        S.op("scalar", lambda h, p=p, ch=ch:
             h.activation(scores[:, ch * CW:(ch + 1) * CW], p[0:8, 0:CW],
                          AF.Identity, bias=gb[:, 0:1]),
             reads=[pk, gb.name], writes=[scores.name])
    ntk = TOK // 128
    for k8 in range(ntk):
        S.op("tensor", lambda h, k8=k8:
             h.transpose(c.pC[:, 0:8], scores[0:8, k8 * 128:(k8 + 1) * 128], c.idn[0:8, 0:8]),
             reads=[scores.name, c.idn.name], writes=["pC"])
        S.op("vector", lambda h: h.reduce_max(c.colsc[:, 0:1], c.pC[:, 0:8], axis=AX.X),
             reads=["pC"], writes=["colsc"])
        S.op("vector", lambda h: h.tensor_scalar(tm1[:, :], c.pC[:, 0:8], c.colsc[:, 0:1],
                                                 None, ALU.is_ge),
             reads=["pC", "colsc"], writes=[tm1.name])
        S.op("vector", lambda h: h.scalar_tensor_tensor(tl2[:, :], tm1[:, :], NEG,
                                                        c.pC[:, 0:8], ALU.mult, ALU.add),
             reads=[tm1.name, "pC"], writes=[tl2.name])
        S.op("vector", lambda h: h.reduce_max(c.colsc[:, 1:2], tl2[:, :], axis=AX.X),
             reads=[tl2.name], writes=["colsc2"])
        S.op("vector", lambda h: h.tensor_scalar(tm2[:, :], tl2[:, :], c.colsc[:, 1:2],
                                                 None, ALU.is_ge),
             reads=[tl2.name, "colsc2"], writes=[tm2.name])
        S.op("vector", lambda h: h.tensor_sub(c.colsc[:, 2:3], c.colsc[:, 0:1], c.colsc[:, 1:2]),
             reads=["colsc", "colsc2"], writes=["colsc3"])
        S.op("scalar", lambda h: h.activation(c.colsc[:, 3:4], c.colsc[:, 2:3], AF.Exp),
             reads=["colsc3"], writes=["colsc4"])
        S.op("vector", lambda h: h.tensor_scalar(c.colsc[:, 4:5], c.colsc[:, 3:4], 1.0,
                                                 None, ALU.add),
             reads=["colsc4"], writes=["colsc5"])
        S.op("vector", lambda h: h.reciprocal(c.colsc[:, 5:6], c.colsc[:, 4:5]),
             reads=["colsc5"], writes=["colsc6"])
        S.op("vector", lambda h: h.tensor_scalar(c.colsc[:, 6:7], c.colsc[:, 5:6], 0.5,
                                                 None, ALU.mult),
             reads=["colsc6"], writes=["colsc7"])
        S.op("vector", lambda h: h.tensor_scalar(c.colsc[:, 7:8], c.colsc[:, 5:6], -0.5,
                                                 0.5, ALU.mult, ALU.add),
             reads=["colsc6"], writes=["colsc8"])
        S.op("vector", lambda h: h.tensor_scalar(tl2[:, :], tm2[:, :], c.colsc[:, 6:7],
                                                 None, ALU.mult),
             reads=[tm2.name, "colsc7"], writes=[tl2.name])
        S.op("vector", lambda h: h.scalar_tensor_tensor(ctok[:, :], tm1[:, :], c.colsc[:, 7:8],
                                                        tl2[:, :], ALU.mult, ALU.add),
             reads=[tm1.name, "colsc8", tl2.name], writes=[ctok.name])
        S.op("tensor", lambda h, k8=k8:
             h.transpose(c.pBF[0:8, k8 * 128:(k8 + 1) * 128], ctok[:, 0:8], c.idnb[0:128, 0:128]),
             reads=[ctok.name, c.idnb.name], writes=["pBF"])
    S.op("scalar", lambda h: h.copy(combb[:, :], c.pBF[0:8, 0:TOK]),
         reads=["pBF"], writes=[combb.name])
    for t in range(NT):
        for ch in range(NCh):
            p = psc[(t * NCh + ch) % 2]
            pk = "pA" if (t * NCh + ch) % 2 == 0 else "pB"
            S.op("tensor", lambda h, p=p, t=t, ch=ch:
                 h.matmul(p[:, 0:CW], b2sb[:, t * 128:(t + 1) * 128],
                          combb[:, ch * CW:(ch + 1) * CW], start=True, stop=True),
                 reads=[b2sb.name, combb.name], writes=[pk])
            S.op("vector", lambda h, p=p, t=t, ch=ch:
                 h.tensor_add(c.x[:, t * TOK + ch * CW: t * TOK + (ch + 1) * CW],
                              c.x[:, t * TOK + ch * CW: t * TOK + (ch + 1) * CW],
                              p[:, 0:CW]),
                 reads=[pk, f"x{t}"], writes=[f"x{t}"])
    w1u_tiles = [2, 2, 2, 2, 2, 1]
    crow = S.sbuf(f"crow{call_id}", [1, TOK], BF16)
    for e in range(NE):
        S.dma("sync", crow[0:1, :], combb[e:e + 1, :], reads=[combb.name], writes=[crow.name])
        for ch in range(NCh):
            S.op("tensor", lambda h, e=e, ch=ch:
                 h.matmul(c.pW0[:, ch * 512: ch * 512 + CW], c.onesb[0:1, :],
                          crow[0:1, ch * CW:(ch + 1) * CW], start=True, stop=True),
                 reads=[crow.name, c.onesb.name], writes=["pW0"])
        S.op("scalar", lambda h: h.copy(cbc[:, 0:TOK], c.pW0[:, 0:TOK]),
             reads=["pW0"], writes=[cbc.name])
        S.dma("gpsimd", b1c[:, :], IN["b1c"][e * 128:(e + 1) * 128, :], writes=[b1c.name])
        h0 = 0
        for ui, uts in enumerate(w1u_tiles):
            wb = c.W[ui % 2]
            wk = wb.name
            cols = NT * 128 * uts // 128
            ncols = NT * uts * 128 // 1
            S.dma("gpsimd", bass.AP(wb, 0, [[c.wcols, 128], [uts * 128, NT], [1, uts * 128]]),
                  bass.AP(IN["w1T"], (e * D * EH + h0 * 128), [[EH, 128], [128 * EH, NT], [1, uts * 128]]),
                  reads=[], writes=[wk])
            for j in range(uts):
                hj = h0 + j
                for ch in range(NCh):
                    p = psc[(hj * NCh + ch) % 2]
                    pk = "pA" if (hj * NCh + ch) % 2 == 0 else "pB"
                    for i in range(NT):
                        S.op("tensor", lambda h, p=p, i=i, j=j, ch=ch, wb=wb, uts=uts:
                             h.matmul(p[:, 0:CW],
                                      wb[:, i * uts * 128 + j * 128: i * uts * 128 + (j + 1) * 128],
                                      c.A[:, i * TOK + ch * CW: i * TOK + (ch + 1) * CW],
                                      start=(i == 0), stop=(i == NT - 1)),
                             reads=[wk, f"A{i}"], writes=[pk])
                    hb = hs[(hj * NCh + ch) % 2]
                    S.op("scalar", lambda h, p=p, j=j, hj=hj, hb=hb:
                         h.activation(hb[:, :], p[:, 0:CW], AF.Silu, bias=b1c[:, hj:hj + 1]),
                         reads=[pk, b1c.name], writes=[hb.name])
                    S.op("vector", lambda h, hj=hj, ch=ch, hb=hb:
                         h.tensor_mul(hfull[:, hj * TOK + ch * CW: hj * TOK + (ch + 1) * CW],
                                      hb[:, :], cbc[:, ch * CW:(ch + 1) * CW]),
                         reads=[hb.name, cbc.name], writes=[f"hf{hj}", "Gbuf"])
            h0 += uts
        for oq in range(8):
            wb = c.W[oq % 2]
            wk = wb.name
            S.dma("gpsimd", bass.AP(wb, 0, [[c.wcols, 128], [256, 11], [1, 256]]),
                  bass.AP(IN["w2T"], (e * EH * D + oq * 256), [[D, 128], [128 * D, 11], [1, 256]]),
                  reads=[], writes=[wk])
            for ot in range(2):
                o = oq * 2 + ot
                for ch in range(NCh):
                    p = psc[(o * NCh + ch) % 2]
                    pk = "pA" if (o * NCh + ch) % 2 == 0 else "pB"
                    for j in range(11):
                        S.op("tensor", lambda h, p=p, j=j, ot=ot, ch=ch, wb=wb:
                             h.matmul(p[:, 0:CW],
                                      wb[:, j * 256 + ot * 128: j * 256 + (ot + 1) * 128],
                                      hfull[:, j * TOK + ch * CW: j * TOK + (ch + 1) * CW],
                                      start=(j == 0), stop=(j == 10)),
                             reads=[wk, f"hf{j}", "Gbuf"], writes=[pk])
                    S.op("vector", lambda h, p=p, o=o, ch=ch:
                         h.tensor_add(c.x[:, o * TOK + ch * CW: o * TOK + (ch + 1) * CW],
                                      c.x[:, o * TOK + ch * CW: o * TOK + (ch + 1) * CW],
                                      p[:, 0:CW]),
                         reads=[pk, f"x{o}"], writes=[f"x{o}"])




def stream_weight_chunks(S, c, IN_name, dram, rows, ocols_total, oc_w, cast=True):
    nch = (ocols_total + oc_w - 1) // oc_w
    for ci in range(nch):
        o0 = ci * oc_w
        ow = min(oc_w, ocols_total - o0)
        wb = c.W[ci % 2]
        S.dma("gpsimd",
              bass.AP(wb, 0, [[c.wcols, 128], [ow, NT], [1, ow]]),
              bass.AP(dram, o0, [[ocols_total, 128], [128 * ocols_total, NT], [1, ow]]),
              writes=[wb.name])
        yield ci, wb, wb.name, o0, ow


def emit_l0attn(S, c, IN):
    TOK, CW, NCh = c.TOK, c.CW, c.NC
    emit_layernorm(S, c, c.A, eps_idx=0, dstname="A")
    mnTf = S.sbuf("sb_mnTf", [128, 64], F32)
    S.dma("gpsimd", bass.AP(mnTf, 0, [[64, 128], [4, 16], [1, 4]]),
          bass.AP(IN["mnT_h"], 0, [[4, 128], [128 * 4, 16], [1, 4]]), writes=[mnTf.name])
    mnT = S.sbuf("sb_mnT", [128, 64], BF16)
    S.op("scalar", lambda h: h.copy(mnT[:, :], mnTf[:, :]), reads=[mnTf.name], writes=[mnT.name])
    bw = S.sbuf("sb_bw", [16, 4], F32)
    S.dma("gpsimd", bw[:, :], IN["bank_w"][:, :], writes=[bw.name])
    memT = c.G
    S.dma("gpsimd", bass.AP(memT, 0, [[c.G.shape[1], 128], [16 * 128, 4], [128, 16], [1, 128]]),
          bass.AP(IN["mem_h"], 0, [[128, 128], [16 * 128 * 128, 4], [128 * 128, 16], [1, 128]]),
          writes=["memT", "Gbuf"])
    wsm = S.sbuf("wsm", [16, 4], F32)
    S.op("vector", lambda h: h.reduce_max(c.colsc[0:16, 0:1], bw[:, :], axis=AX.X, negate=True),
         reads=[bw.name], writes=["colsc"])
    S.op("scalar", lambda h: h.activation(wsm[:, :], bw[:, :], AF.Exp, bias=c.colsc[0:16, 0:1]),
         reads=[bw.name, "colsc"], writes=[wsm.name])
    S.op("vector", lambda h: h.reduce_sum(c.colsc[0:16, 1:2], wsm[:, :], axis=AX.X),
         reads=[wsm.name], writes=["colsc2"])
    S.op("vector", lambda h: h.reciprocal(c.colsc[0:16, 2:3], c.colsc[0:16, 1:2]),
         reads=["colsc2"], writes=["colsc3"])
    S.op("vector", lambda h: h.tensor_scalar(wsm[:, :], wsm[:, :], c.colsc[0:16, 2:3],
                                             None, ALU.mult),
         reads=[wsm.name, "colsc3"], writes=[wsm.name])
    wrow = S.sbuf("wrow", [1, 64], F32)
    S.dma("sync", IN["bounce"][0:16, 0:4], wsm[:, :], reads=[wsm.name], writes=["bounce"])
    S.dma("sync", bass.AP(wrow, 0, [[64, 1], [16, 4], [1, 16]]),
          bass.AP(IN["bounce"], 0, [[64, 1], [1, 4], [4, 16]]),
          reads=["bounce"], writes=[wrow.name], allow_slow_non_contiguous=True)
    lnw = S.sbuf("lnw", [1, 64], F32)
    S.op("scalar", lambda h: h.activation(lnw[:, :], wrow[:, :], AF.Ln),
         reads=[wrow.name], writes=[lnw.name])
    mnsf = S.sbuf("mns4f", [128, 4], F32)
    mns = S.sbuf("mns4", [128, 4], BF16)
    S.op("vector", lambda h: h.reduce_sum(
        mnsf[:, 0:4], bass.AP(mnT, 0, [[64, 128], [1, 4], [4, 16]]), axis=AX.X),
        reads=[mnT.name], writes=[mnsf.name])
    S.op("scalar", lambda h: h.copy(mns[:, 0:4], mnsf[:, 0:4]),
         reads=[mnsf.name], writes=[mns.name])
    acr = S.sbuf("acr", [1, 4], F32)
    S.op("tensor", lambda h: h.matmul(c.pC[0:1, 0:4], c.invdcol[:, 0:1], mns[:, 0:4],
                                      start=True, stop=True),
         reads=[mns.name, c.invdcol.name], writes=["pC"])
    S.op("vector", lambda h: h.tensor_scalar(acr[0:1, 0:4], c.pC[0:1, 0:4],
                                             1e-6 / 2048.0, None, ALU.is_ge),
         reads=["pC"], writes=[acr.name])
    S.op("tensor", lambda h: h.matmul(c.pC[:, 0:4], c.onesf[0:1, :], acr[0:1, 0:4],
                                      start=True, stop=True),
         reads=[acr.name, c.onesf.name], writes=["pC"])
    ac128 = S.sbuf("ac128", [128, 4], F32)
    S.op("scalar", lambda h: h.copy(ac128[:, :], c.pC[:, 0:4]),
         reads=["pC"], writes=[ac128.name])
    for n in range(4):
        S.op("vector", lambda h, n=n: h.tensor_scalar(
            memT[:, n * 2048:(n + 1) * 2048], memT[:, n * 2048:(n + 1) * 2048],
            ac128[:, n:n + 1], None, ALU.mult),
            reads=["memT", ac128.name, "Gbuf"], writes=["memT", "Gbuf"])
    for ci, wb, wk, o0, ow in stream_weight_chunks(S, c, "wqT", IN["wqT"], D, D, 256):
        for ot2 in range(ow // 128):
            ho = (o0 + ot2 * 128) // 128
            for ch in range(NCh):
                p = (c.pA, c.pB)[(ho * NCh + ch) % 2]
                pk = "pA" if (ho * NCh + ch) % 2 == 0 else "pB"
                for i in range(NT):
                    S.op("tensor", lambda h, p=p, i=i, ot2=ot2, ch=ch, wb=wb, ow=ow:
                         h.matmul(p[:, 0:CW], wb[:, i * ow + ot2 * 128: i * ow + (ot2 + 1) * 128],
                                  c.A[:, i * TOK + ch * CW: i * TOK + (ch + 1) * CW],
                                  start=(i == 0), stop=(i == NT - 1)),
                         reads=[wk, f"A{i}"], writes=[pk])
                t0, t1 = c.t32
                S.op("vector", lambda h, p=p, t0=t0: h.tensor_scalar(t0[:, :], p[:, 0:CW],
                                                                    0.0, None, ALU.min),
                     reads=[pk], writes=[t0.name])
                S.op("scalar", lambda h, t0=t0, t1=t1: h.activation(t1[:, :], t0[:, :], AF.Exp),
                     reads=[t0.name], writes=[t1.name])
                S.op("vector", lambda h, p=p, t0=t0: h.tensor_sub(t0[:, :], p[:, 0:CW], t0[:, :]),
                     reads=[pk, t0.name], writes=[t0.name])
                S.op("vector", lambda h, ho=ho, ch=ch, t0=t0, t1=t1:
                     h.tensor_add(c.B[:, ho * TOK + ch * CW: ho * TOK + (ch + 1) * CW],
                                  t0[:, :], t1[:, :]),
                     reads=[t0.name, t1.name], writes=[f"B{ho}"])
    nbs = c.r0
    rps = (S.sbuf("rp_s0", [1, TOK], BF16), S.sbuf("rp_s1", [1, TOK], BF16))
    sgs = [("sgs0", 64 * 128), ("sgs1", 64 * 128 + TOK)]
    for h_ in range(16):
        for n in range(4):
            pw, pwk = (c.pW0, "pW0") if n % 2 == 0 else (c.pW1, "pW1")
            for ch in range(NCh):
                S.op("tensor", lambda hh, n=n, h_=h_, ch=ch:
                     hh.matmul(c.pA[0:1, 0:CW],
                               mnT[:, h_ * 4 + n: h_ * 4 + n + 1],
                               c.B[:, h_ * TOK + ch * CW: h_ * TOK + (ch + 1) * CW],
                               start=True, stop=True),
                     reads=[mnT.name, f"B{h_}"], writes=["pA"])
                S.op("vector", lambda hh, ch=ch:
                     hh.tensor_scalar(nbs[0:1, ch * CW:(ch + 1) * CW],
                                      c.pA[0:1, 0:CW], 1e-6, None, ALU.max),
                     reads=["pA"], writes=[nbs.name])
            S.op("scalar", lambda hh: hh.activation(nbs[0:1, :], nbs[0:1, :], AF.Ln),
                 reads=[nbs.name], writes=[nbs.name])
            rp = rps[n % 2]
            S.op("scalar", lambda hh, n=n, h_=h_, rp=rp:
                 hh.activation(rp[0:1, :], nbs[0:1, :], AF.Exp, scale=-1.0,
                               bias=lnw[0:1, n * 16 + h_: n * 16 + h_ + 1]),
                 reads=[nbs.name, lnw.name], writes=[rp.name])
            for ch in range(NCh):
                S.op("tensor", lambda hh, pw=pw, ch=ch, rp=rp:
                     hh.matmul(pw[:, ch * CW:(ch + 1) * CW], c.onesb[0:1, :],
                               rp[0:1, ch * CW:(ch + 1) * CW],
                               start=True, stop=True),
                     reads=[rp.name, c.onesb.name], writes=[pwk])
            gk, goff = sgs[n % 2]
            S.op("vector", lambda hh, pw=pw, h_=h_, goff=goff:
                 hh.tensor_mul(c.G[:, goff:goff + TOK],
                               c.B[:, h_ * TOK:(h_ + 1) * TOK], pw[:, 0:TOK]),
                 reads=[f"B{h_}", pwk], writes=[gk, "Gbuf"])
            for ch in range(NCh):
                p2, p2k = (c.pB, "pB") if ch == 0 else (c.pC, "pC")
                S.op("tensor", lambda hh, p2=p2, n=n, h_=h_, ch=ch, goff=goff:
                     hh.matmul(p2[:, 0:CW], memT[:, (n * 16 + h_) * 128:(n * 16 + h_ + 1) * 128],
                               c.G[:, goff + ch * CW: goff + (ch + 1) * CW],
                               start=(n == 0), stop=(n == 3)),
                     reads=["memT", gk, "Gbuf"], writes=[p2k])
        for ch in range(NCh):
            p2, p2k = (c.pB, "pB") if ch == 0 else (c.pC, "pC")
            S.op("scalar", lambda hh, p2=p2, h_=h_, ch=ch:
                 hh.copy(c.A[:, h_ * TOK + ch * CW: h_ * TOK + (ch + 1) * CW], p2[:, 0:CW]),
                 reads=[p2k], writes=[f"A{h_}"])
    for ci, wb, wk, o0, ow in stream_weight_chunks(S, c, "woT", IN["woT"], D, D, 256):
        for ot2 in range(ow // 128):
            o = (o0 + ot2 * 128) // 128
            for ch in range(NCh):
                p = (c.pA, c.pW1)[(o * NCh + ch) % 2]
                pk = "pA" if (o * NCh + ch) % 2 == 0 else "pW1"
                for i in range(NT):
                    S.op("tensor", lambda h, p=p, i=i, ot2=ot2, ch=ch, wb=wb, ow=ow:
                         h.matmul(p[:, 0:CW], wb[:, i * ow + ot2 * 128: i * ow + (ot2 + 1) * 128],
                                  c.A[:, i * TOK + ch * CW: i * TOK + (ch + 1) * CW],
                                  start=(i == 0), stop=(i == NT - 1)),
                         reads=[wk, f"A{i}"], writes=[pk])
                S.op("vector", lambda h, p=p, o=o, ch=ch:
                     h.tensor_add(c.x[:, o * TOK + ch * CW: o * TOK + (ch + 1) * CW],
                                  c.x[:, o * TOK + ch * CW: o * TOK + (ch + 1) * CW],
                                  p[:, 0:CW]),
                     reads=[pk, f"x{o}"], writes=[f"x{o}"])




def bcast16(S, c, IN, row_dram, name):
    if not hasattr(c, "_bcrow"):
        c._bcrow = S.sbuf("bc_row", [1, D], F32)
    row = c._bcrow
    S.dma("gpsimd", row[:, :], row_dram[:, :], writes=[row.name])
    out = S.sbuf(name, [16, D], BF16)
    for ch in range(4):
        S.op("tensor", lambda h, ch=ch:
             h.matmul(c.pC[0:16, 0:512], c.onesf[0:1, 0:16],
                      row[0:1, ch * 512:(ch + 1) * 512], start=True, stop=True),
             reads=[row.name, c.onesf.name], writes=["pC"])
        S.op("scalar", lambda h, ch=ch: h.copy(out[:, ch * 512:(ch + 1) * 512],
                                               c.pC[0:16, 0:512]),
             reads=["pC"], writes=[out.name])
    return out


def emit_l1ffn_prologue(S, c, IN, want_fin=True):
    P = {}
    cwf = S.sbuf("cwf", [128, NT * 16], F32)
    S.dma("gpsimd", bass.AP(cwf, 0, [[NT * 16, 128], [16, NT], [1, 16]]),
          bass.AP(IN["cwT_h"], 0, [[16, 128], [128 * 16, NT], [1, 16]]), writes=[cwf.name])
    npw = S.sbuf("npw_c", [128, 16], F32)
    S.dma("gpsimd", npw[:, :], IN["npw_cols"][:, :], writes=[npw.name])
    cwb = S.sbuf("cwb", [128, NT * 16], BF16)
    for t in range(NT):
        S.op("vector", lambda h, t=t: h.tensor_scalar(
            cwb[:, t * 16:(t + 1) * 16], cwf[:, t * 16:(t + 1) * 16],
            npw[:, t:t + 1], None, ALU.mult),
            reads=[cwf.name, npw.name], writes=[cwb.name])
    P["cwb"] = cwb
    cb = S.sbuf("cb_col", [16, 1], F32)
    S.dma("gpsimd", cb[:, :], IN["cb"][:, :], writes=[cb.name])
    P["cb"] = cb
    if not want_fin:
        P["rstd16"] = S.sbuf("rstd16", [16, c.TOK], F32)
        P["lg"] = S.sbuf("lg", [16, c.TOK], F32)
        P["vals4"] = S.sbuf("vals4", [16, 4], F32)
        P["masks"] = [S.sbuf(f"mk{k}", [16, c.TOK], BF16) for k in range(4)]
        P["lgt"] = (P["lg"], P["rstd16"])
        P["mT"] = S.sbuf("mT_sb", [128, 4 * (c.TOK // 128) * 16], BF16)
        P["rstage"] = S.sbuf("rstage", [16, 256], F32)
        wxb0 = S.sbuf("wxb0", [128, 256], BF16)
        P["wxb"] = (wxb0, wxb0)
        return P
    npc = S.sbuf("npost_c", [128, 16], F32)
    S.dma("gpsimd", npc[:, :], IN["npost_cols"][:, :], writes=[npc.name])
    P["npost_c"] = npc
    for t, nm in [("bq", "bq_cols"), ("bk", "bk_cols"), ("opb", "opb_cols")]:
        tt = S.sbuf(t + "_c", [128, 16], F32)
        S.dma("gpsimd", tt[:, :], IN[nm][:, :], writes=[tt.name])
        P[t] = tt
    bqs = S.sbuf("bqs_c", [128, 16], F32)
    S.op("vector", lambda h: h.tensor_scalar(bqs[:, :], P["bq"][:, :], 0.125, None, ALU.mult),
         reads=[P["bq"].name], writes=[bqs.name])
    P["bqs"] = bqs
    P["npost16"] = bcast16(S, c, IN, IN["npost_row"], "npost16")
    P["bv16"] = bcast16(S, c, IN, IN["bv_row"], "bv16")
    P["wb16"] = bcast16(S, c, IN, IN["wb_row"], "wb16")
    P["rstd16"] = S.sbuf("rstd16", [16, c.TOK], F32)
    P["wr"] = S.sbuf("wr_acc", [16, D], F32)
    P["lg"] = P["wr"]
    P["vals4"] = S.sbuf("vals4", [16, 4], F32)
    P["masks"] = [S.sbuf(f"mk{k}", [16, c.TOK], BF16) for k in range(4)]
    P["lgt"] = (P["lg"], P["rstd16"])
    P["mT"] = S.sbuf("mT_sb", [128, 4 * (c.TOK // 128) * 16], BF16)
    P["rstage"] = S.sbuf("rstage", [16, 256], F32)
    wxb0 = S.sbuf("wxb0", [128, 256], BF16)
    P["wxb"] = (wxb0, wxb0)
    return P


def emit_l1ffn_pre(S, c, IN, P, xb_ready=False):
    TOK, CW, NCh = c.TOK, c.CW, c.NC
    ntk = TOK // 128
    emit_norm_stats(S, c, "rms", 1)
    for ch in range(NCh):
        S.op("tensor", lambda h, ch=ch: h.matmul(c.pC[0:16, 0:CW], c.onesf[0:1, 0:16],
                                                 c.r1[0:1, ch * CW:(ch + 1) * CW],
                                                 start=True, stop=True),
             reads=[c.r1.name, c.onesf.name], writes=["pC"])
        S.op("scalar", lambda h, ch=ch: h.copy(P["rstd16"][:, ch * CW:(ch + 1) * CW],
                                               c.pC[0:16, 0:CW]),
             reads=["pC"], writes=[P["rstd16"].name])
    lg = P["lg"]
    for ch in range(NCh):
        p, pk = (c.pA, "pA") if ch % 2 == 0 else (c.pB, "pB")
        for t in range(NT):
            S.op("tensor", lambda h, p=p, t=t, ch=ch: h.matmul(
                p[0:16, 0:CW], P["cwb"][:, t * 16:(t + 1) * 16],
                c.x[:, t * TOK + ch * CW: t * TOK + (ch + 1) * CW],
                start=(t == 0), stop=(t == NT - 1)),
                reads=[P["cwb"].name, f"x{t}"], writes=[pk])
        S.op("vector", lambda h, p=p, ch=ch: h.tensor_mul(
            lg[:, ch * CW:(ch + 1) * CW], p[0:16, 0:CW],
            P["rstd16"][:, ch * CW:(ch + 1) * CW]),
            reads=[pk, P["rstd16"].name], writes=[lg.name])
    S.op("vector", lambda h: h.tensor_scalar(lg[:, 0:TOK], lg[:, 0:TOK], P["cb"][:, 0:1],
                                             None, ALU.add),
         reads=[lg.name, P["cb"].name], writes=[lg.name])
    cur, nxt = P["lgt"]
    for k in range(4):
        S.op("vector", lambda h, k=k, cur=cur: h.reduce_max(P["vals4"][:, k:k + 1], cur[:, 0:TOK], axis=AX.X),
             reads=[cur.name], writes=[f"v4_{k}"])
        S.op("vector", lambda h, k=k, cur=cur: h.tensor_scalar(P["masks"][k][:, :], cur[:, 0:TOK],
                                                      P["vals4"][:, k:k + 1], None, ALU.is_ge),
             reads=[cur.name, f"v4_{k}"], writes=[P["masks"][k].name])
        if k < 3:
            S.op("vector", lambda h, k=k, cur=cur, nxt=nxt: h.scalar_tensor_tensor(
                nxt[:, 0:TOK], P["masks"][k][:, :], NEG, cur[:, 0:TOK], ALU.mult, ALU.add),
                reads=[P["masks"][k].name, cur.name], writes=[nxt.name])
            cur, nxt = nxt, cur
    for k in range(4):
        for tt in range(ntk):
            S.op("tensor", lambda h, k=k, tt=tt: h.transpose(
                c.pBF[:, tt * 16:(tt + 1) * 16], P["masks"][k][0:16, tt * 128:(tt + 1) * 128],
                c.idnb[0:16, 0:16]),
                reads=[P["masks"][k].name, c.idnb.name], writes=["pBF"])
        S.op("scalar", lambda h, k=k: h.copy(
            P["mT"][:, k * ntk * 16:(k + 1) * ntk * 16], c.pBF[:, 0:ntk * 16]),
            reads=["pBF"], writes=[P["mT"].name])
    rows_ps = [(c.pW0, "pW0", 0), (c.pW0, "pW0", 512), (c.pW1, "pW1", 0), (c.pW1, "pW1", 512)]
    for ci, wb, wk, o0, ow in stream_weight_chunks(S, c, "wwT", IN["wwT"], D, D, 256):
        for tt in range(ntk):
            p, pk = (c.pA, "pA") if tt % 2 == 0 else (c.pB, "pB")
            for i in range(NT):
                S.op("tensor", lambda h, p=p, i=i, tt=tt, wb=wb, ow=ow: h.matmul(
                    p[:, 0:ow], c.x[:, i * TOK + tt * 128:i * TOK + (tt + 1) * 128],
                    wb[:, i * ow:(i + 1) * ow], start=(i == 0), stop=(i == NT - 1)),
                    reads=[wk, f"x{i}"], writes=[pk])
            wx = P["wxb"][tt % 2]
            S.op("scalar", lambda h, p=p, wx=wx, ow=ow: h.copy(wx[:, 0:ow], p[:, 0:ow]),
                 reads=[pk], writes=[wx.name])
            for k in range(4):
                rp, rpk, rof = rows_ps[k]
                S.op("tensor", lambda h, k=k, tt=tt, wx=wx, rp=rp, rof=rof, ow=ow: h.matmul(
                    rp[0:16, rof:rof + ow],
                    P["mT"][:, (k * ntk + tt) * 16:(k * ntk + tt + 1) * 16],
                    wx[:, 0:ow], start=(tt == 0), stop=(tt == ntk - 1)),
                    reads=[P["mT"].name, wx.name], writes=[rpk])
        for k in range(4):
            rp, rpk, rof = rows_ps[k]
            S.op("vector", lambda h, k=k, rp=rp, rof=rof, ow=ow: h.tensor_copy(
                P["rstage"][:, 0:ow], rp[0:16, rof:rof + ow]),
                reads=[rpk], writes=[P["rstage"].name])
            S.dma("sync", IN["pay_out"][:, 4 + k * D + o0: 4 + k * D + o0 + ow],
                  P["rstage"][:, 0:ow], reads=[P["rstage"].name], writes=["pay_out"])
    S.dma("sync", IN["pay_out"][:, 0:4], P["vals4"][:, :],
          reads=["v4_0", "v4_1", "v4_2", "v4_3"], writes=["pay_out"])


def emit_l1ffn_finish(S, c, IN, P, CPB, first_iter):
    TOK, CW, NCh = c.TOK, c.CW, c.NC
    ntk = TOK // 128
    NCAND = 4 * CPB
    wsin = S.sbuf("wsin_sb", [16, D], F32)
    S.dma("gpsimd", wsin[:, :], IN["wsin"][:, :], writes=[wsin.name])
    PW = 4 + 4 * D
    gv = S.sbuf("gv", [16, NCAND], F32)
    for cc in range(CPB):
        S.dma("gpsimd", gv[:, cc * 4:(cc + 1) * 4],
              IN["pay_in"][:, cc * PW: cc * PW + 4], writes=[gv.name])
    gcur = S.sbuf("gcur", [16, NCAND], F32)
    gnxt = S.sbuf("gnxt", [16, NCAND], F32)
    gvals = S.sbuf("gvals", [16, 4], F32)
    gmask = [S.sbuf(f"gm{k}", [16, NCAND], F32) for k in range(4)]
    S.op("vector", lambda h: h.tensor_copy(gcur[:, :], gv[:, :]),
         reads=[gv.name], writes=[gcur.name])
    cur, nxt = gcur, gnxt
    for k in range(4):
        S.op("vector", lambda h, k=k, cur=cur: h.reduce_max(gvals[:, k:k + 1], cur[:, :], axis=AX.X),
             reads=[cur.name], writes=[f"gv4_{k}"])
        S.op("vector", lambda h, k=k, cur=cur: h.tensor_scalar(gmask[k][:, :], cur[:, :],
                                                               gvals[:, k:k + 1], None, ALU.is_ge),
             reads=[cur.name, f"gv4_{k}"], writes=[gmask[k].name])
        if k < 3:
            S.op("vector", lambda h, k=k, cur=cur, nxt=nxt: h.scalar_tensor_tensor(
                nxt[:, :], gmask[k][:, :], NEG, cur[:, :], ALU.mult, ALU.add),
                reads=[gmask[k].name, cur.name], writes=[nxt.name])
            cur, nxt = nxt, cur
    d4 = S.sbuf("d4", [16, 4], F32)
    S.op("vector", lambda h: h.tensor_scalar(d4[:, :], gvals[:, :], gvals[:, 0:1],
                                             None, ALU.subtract),
         reads=["gv4_0", "gv4_1", "gv4_2", "gv4_3"], writes=[d4.name])
    e4 = S.sbuf("e4", [16, 4], F32)
    S.op("scalar", lambda h: h.activation(e4[:, :], d4[:, :], AF.Exp),
         reads=[d4.name], writes=[e4.name])
    S.op("vector", lambda h: h.reduce_sum(d4[:, 0:1], e4[:, :], axis=AX.X),
         reads=[e4.name], writes=["dsum"])
    S.op("vector", lambda h: h.reciprocal(d4[:, 1:2], d4[:, 0:1]),
         reads=["dsum"], writes=["drec"])
    w4 = S.sbuf("w4", [16, 4], F32)
    S.op("vector", lambda h: h.tensor_scalar(w4[:, :], e4[:, :], d4[:, 1:2], None, ALU.mult),
         reads=[e4.name, "drec"], writes=[w4.name])
    w16 = S.sbuf("w16", [16, NCAND], F32)
    S.op("vector", lambda h: h.tensor_scalar(w16[:, :], gmask[0][:, :], w4[:, 0:1],
                                             None, ALU.mult),
         reads=[gmask[0].name, w4.name], writes=[w16.name])
    for k in range(1, 4):
        S.op("vector", lambda h, k=k: h.scalar_tensor_tensor(
            w16[:, :], gmask[k][:, :], w4[:, k:k + 1], w16[:, :], ALU.mult, ALU.add),
            reads=[gmask[k].name, w4.name, w16.name], writes=[w16.name])
    wr = P["wr"]
    rowb0 = S.sbuf("rowb0", [16, D], F32)
    rowb = (rowb0, rowb0)
    first = True
    for cc in range(CPB):
        for k in range(4):
            j = cc * 4 + k
            rb = rowb[j % 2]
            S.dma("gpsimd", rb[:, :],
                  IN["pay_in"][:, cc * PW + 4 + k * D: cc * PW + 4 + (k + 1) * D],
                  writes=[rb.name])
            if first:
                S.op("vector", lambda h, j=j, rb=rb: h.tensor_scalar(
                    wr[:, :], rb[:, :], w16[:, j:j + 1], None, ALU.mult),
                    reads=[rb.name, w16.name], writes=[wr.name])
                first = False
            else:
                S.op("vector", lambda h, j=j, rb=rb: h.scalar_tensor_tensor(
                    wr[:, :], rb[:, :], w16[:, j:j + 1], wr[:, :], ALU.mult, ALU.add),
                    reads=[rb.name, w16.name, wr.name], writes=[wr.name])
    S.op("vector", lambda h: h.tensor_add(wr[:, :], wr[:, :], P["wb16"][:, :]),
         reads=[wr.name, P["wb16"].name], writes=[wr.name])
    wst = wr
    S.op("vector", lambda h: h.tensor_add(wst[:, :], wsin[:, :], wr[:, :]),
         reads=[wsin.name, wr.name], writes=[wst.name])
    ssq = rowb[0]
    S.op("vector", lambda h: h.tensor_mul(ssq[:, :], wst[:, :], wst[:, :]),
         reads=[wst.name], writes=[ssq.name])
    rr = S.sbuf("rrow16", [16, 1], F32)
    S.op("vector", lambda h: h.reduce_sum(rr[:, 0:1], ssq[:, :], axis=AX.X),
         reads=[ssq.name], writes=[rr.name])
    S.op("scalar", lambda h: h.activation(rr[:, 0:1], rr[:, 0:1], AF.Ln,
                                          scale=1.0 / D, bias=c.epscol[0:16, 1:2]),
         reads=[rr.name, c.epscol.name], writes=[rr.name])
    S.op("scalar", lambda h: h.activation(rr[:, 0:1], rr[:, 0:1], AF.Exp, scale=-0.5),
         reads=[rr.name], writes=[rr.name])
    wss = wsin
    S.op("vector", lambda h: h.scalar_tensor_tensor(wss[:, :], wst[:, :], rr[:, 0:1],
                                                    P["npost16"][:, :], ALU.mult, ALU.mult),
         reads=[wst.name, rr.name, P["npost16"].name], writes=["wss"])
    wsb = S.sbuf("wsb_bf", [16, D], BF16)
    S.op("scalar", lambda h: h.copy(wsb[:, :], wss[:, :]), reads=["wss"], writes=[wsb.name])
    wsT = S.sbuf("wsT", [128, NT * 16], BF16)
    for i in range(NT):
        S.op("tensor", lambda h, i=i: h.transpose(
            c.pBF[:, i * 16:(i + 1) * 16], wsb[0:16, i * 128:(i + 1) * 128],
            c.idnb[0:16, 0:16]),
            reads=[wsb.name, c.idnb.name], writes=["pBF"])
    S.op("scalar", lambda h: h.copy(wsT[:, :], c.pBF[:, 0:NT * 16]),
         reads=["pBF"], writes=[wsT.name])
    kkb = S.sbuf("kkb", [128, 16 * 16], BF16)
    for ci, wb, wk, o0, ow in stream_weight_chunks(S, c, "wkT1", IN["wkT1"], D, D, 256):
        for ot2 in range(ow // 128):
            o = (o0 + ot2 * 128) // 128
            S.op("tensor", lambda h, wb=wb, ot2=ot2, ow=ow, o=o: h.matmul(
                c.pC[:, 0:16], wb[:, 0 * ow + ot2 * 128:0 * ow + (ot2 + 1) * 128],
                wsT[:, 0 * 16:(0 + 1) * 16], start=True, stop=False),
                reads=[wk, wsT.name], writes=["pC"]) if False else None
            for i in range(NT):
                S.op("tensor", lambda h, wb=wb, i=i, ot2=ot2, ow=ow: h.matmul(
                    c.pC[:, 0:16], wb[:, i * ow + ot2 * 128: i * ow + (ot2 + 1) * 128],
                    wsT[:, i * 16:(i + 1) * 16], start=(i == 0), stop=(i == NT - 1)),
                    reads=[wk, wsT.name], writes=["pC"])
            S.op("scalar", lambda h, o=o: h.activation(kkb[:, o * 16:(o + 1) * 16],
                                                       c.pC[:, 0:16], AF.Identity,
                                                       bias=P["bk"][:, o:o + 1]),
                 reads=["pC", P["bk"].name], writes=[kkb.name])
    vvb = S.sbuf("vvb", [16, D], BF16)
    for ci, wb, wk, o0, ow in stream_weight_chunks(S, c, "wvT1", IN["wvT1"], D, D, 256):
        for i in range(NT):
            S.op("tensor", lambda h, wb=wb, i=i, ow=ow: h.matmul(
                c.pC[0:16, 0:ow], wsT[:, i * 16:(i + 1) * 16],
                wb[:, i * ow:(i + 1) * ow], start=(i == 0), stop=(i == NT - 1)),
                reads=[wk, wsT.name], writes=["pC"])
        S.op("vector", lambda h, o0=o0, ow=ow: h.tensor_add(
            vvb[:, o0:o0 + ow], c.pC[0:16, 0:ow], P["bv16"][:, o0:o0 + ow]),
            reads=["pC", P["bv16"].name], writes=[vvb.name])
    for ci, wb, wk, o0, ow in stream_weight_chunks(S, c, "wqT1", IN["wqT1"], D, D, 256):
        for ot2 in range(ow // 128):
            o = (o0 + ot2 * 128) // 128
            for ch in range(NCh):
                p, pk = (c.pA, "pA") if (o * NCh + ch) % 2 == 0 else (c.pB, "pB")
                for i in range(NT):
                    S.op("tensor", lambda h, p=p, wb=wb, i=i, ot2=ot2, ch=ch, ow=ow: h.matmul(
                        p[:, 0:CW], wb[:, i * ow + ot2 * 128: i * ow + (ot2 + 1) * 128],
                        c.x[:, i * TOK + ch * CW: i * TOK + (ch + 1) * CW],
                        start=(i == 0), stop=(i == NT - 1)),
                        reads=[wk, f"x{i}"], writes=[pk])
                S.op("scalar", lambda h, p=p, o=o, ch=ch: h.activation(
                    c.B[:, o * TOK + ch * CW: o * TOK + (ch + 1) * CW], p[:, 0:CW],
                    AF.Identity, scale=0.125, bias=P["bqs"][:, o:o + 1]),
                    reads=[pk, P["bqs"].name], writes=[f"B{o}"])
    att_e = S.sbuf("att_e", [128, 128], F32)
    att_m = S.sbuf("att_m", [128, 8], F32)
    att_p = S.sbuf("att_p", [128, 128], BF16)
    pT = S.sbuf("pT_sb", [16, c.TOK], BF16)
    for ot in range(NT):
        for hh in range(2):
            hd_ = 2 * ot + hh
            for tt in range(ntk):
                S.op("tensor", lambda h, ot=ot, hh=hh, tt=tt: h.matmul(
                    c.pC[:, tt * 16:(tt + 1) * 16],
                    c.B[64 * hh:64 * hh + 64, ot * TOK + tt * 128: ot * TOK + (tt + 1) * 128],
                    kkb[64 * hh:64 * hh + 64, ot * 16:(ot + 1) * 16],
                    start=True, stop=True),
                    reads=[f"B{ot}", kkb.name], writes=["pC"])
            S.op("vector", lambda h: h.reduce_max(
                att_m[:, 0:ntk],
                bass.AP(c.pC.handle if hasattr(c.pC, 'handle') else c.pC, 0,
                        [[512, 128], [16, ntk], [1, 16]]), axis=AX.X),
                reads=["pC"], writes=[att_m.name])
            S.op("vector", lambda h: h.tensor_sub(
                att_e[:, 0:ntk * 16], c.pC[:, 0:ntk * 16],
                bass.AP(att_m, 0, [[8, 128], [1, ntk], [0, 16]])),
                reads=["pC", att_m.name], writes=[att_e.name])
            S.op("scalar", lambda h: h.activation(att_e[:, 0:ntk * 16], att_e[:, 0:ntk * 16],
                                                  AF.Exp),
                 reads=[att_e.name], writes=[att_e.name])
            S.op("vector", lambda h: h.reduce_sum(
                att_m[:, 0:ntk],
                bass.AP(att_e, 0, [[128, 128], [16, ntk], [1, 16]]), axis=AX.X),
                reads=[att_e.name], writes=[att_m.name])
            S.op("vector", lambda h: h.reciprocal(att_m[:, 0:ntk], att_m[:, 0:ntk]),
                 reads=[att_m.name], writes=[att_m.name])
            S.op("vector", lambda h: h.tensor_mul(
                att_p[:, 0:ntk * 16], att_e[:, 0:ntk * 16],
                bass.AP(att_m, 0, [[8, 128], [1, ntk], [0, 16]])),
                reads=[att_e.name, att_m.name], writes=[att_p.name])
            for tt in range(ntk):
                S.op("tensor", lambda h, tt=tt: h.transpose(
                    c.pBF[0:16, tt * 128:(tt + 1) * 128], att_p[:, tt * 16:(tt + 1) * 16],
                    c.idnb[0:128, 0:128]),
                    reads=[att_p.name, c.idnb.name], writes=["pBF"])
            S.op("scalar", lambda h: h.copy(pT[:, 0:TOK], c.pBF[0:16, 0:TOK]),
                 reads=["pBF"], writes=[pT.name])
            for ch in range(NCh):
                pctx, pctxk = ((c.pA, "pA"), (c.pB, "pB"))[ch % 2]
                S.op("tensor", lambda h, hd_=hd_, hh=hh, ch=ch, pctx=pctx: h.matmul(
                    pctx[64 * hh:64 * hh + 64, 0:CW], vvb[0:16, hd_ * 64:(hd_ + 1) * 64],
                    pT[0:16, ch * CW:(ch + 1) * CW], start=True, stop=True),
                    reads=[vvb.name, pT.name], writes=[pctxk])
        for ch in range(NCh):
            pctx, pctxk = ((c.pA, "pA"), (c.pB, "pB"))[ch % 2]
            S.op("scalar", lambda h, ot=ot, ch=ch, pctx=pctx: h.copy(
                c.A[:, ot * TOK + ch * CW: ot * TOK + (ch + 1) * CW], pctx[:, 0:CW]),
                reads=[pctxk], writes=[f"A{ot}"])
    for ci, wb, wk, o0, ow in stream_weight_chunks(S, c, "opT", IN["opT"], D, D, 256):
        for ot2 in range(ow // 128):
            o = (o0 + ot2 * 128) // 128
            for ch in range(NCh):
                p, pk = (c.pB, "pB") if (o * NCh + ch) % 2 == 0 else (c.pW0, "pW0")
                for i in range(NT):
                    S.op("tensor", lambda h, p=p, wb=wb, i=i, ot2=ot2, ch=ch, ow=ow: h.matmul(
                        p[:, 0:CW], wb[:, i * ow + ot2 * 128: i * ow + (ot2 + 1) * 128],
                        c.A[:, i * TOK + ch * CW: i * TOK + (ch + 1) * CW],
                        start=(i == 0), stop=(i == NT - 1)),
                        reads=[wk, f"A{i}"], writes=[pk])
                S.op("vector", lambda h, p=p, o=o, ch=ch: h.scalar_tensor_tensor(
                    c.x[:, o * TOK + ch * CW: o * TOK + (ch + 1) * CW], p[:, 0:CW],
                    P["opb"][:, o:o + 1], c.x[:, o * TOK + ch * CW: o * TOK + (ch + 1) * CW],
                    ALU.add, ALU.add),
                    reads=[pk, P["opb"].name, f"x{o}"], writes=[f"x{o}"])
    emit_norm_stats(S, c, "rms", 1)
    emit_bcast_row(S, c, c.r1, c.pW1, "pW1", c.r1.name, TOK)
    for t in range(NT):
        for ch in range(NCh):
            S.op("vector", lambda h, t=t, ch=ch: h.scalar_tensor_tensor(
                c.x[:, t * TOK + ch * CW: t * TOK + (ch + 1) * CW],
                c.x[:, t * TOK + ch * CW: t * TOK + (ch + 1) * CW],
                P["npost_c"][:, t:t + 1], c.pW1[:, ch * CW:(ch + 1) * CW],
                ALU.mult, ALU.mult),
                reads=[f"x{t}", P["npost_c"].name, "pW1"], writes=[f"x{t}"])
    return {"wss": wss}


def emit_l1attn(S, c, IN):
    TOK, CW, NCh = c.TOK, c.CW, c.NC
    HTOK = 512
    KEYS = HTOK + TOK
    nqt = TOK // 128
    emit_layernorm(S, c, c.A, eps_idx=0, dstname="A")
    Hx = S.sbuf("Hx", [128, NT * HTOK], BF16)
    for t in range(NT):
        S.dma("gpsimd", Hx[:, t * HTOK:(t + 1) * HTOK],
              IN["xhalo"][t * 128:(t + 1) * 128, :], writes=[f"Hx{t}"])
    for t in range(NT):
        sq = c.sqb[t % 2]
        S.op("vector", lambda h, t=t, sq=sq: h.tensor_mul(
            sq[:, 0:HTOK], Hx[:, t * HTOK:(t + 1) * HTOK], Hx[:, t * HTOK:(t + 1) * HTOK]),
            reads=[f"Hx{t}"], writes=[sq.name])
        S.op("tensor", lambda h, t=t: h.matmul(
            c.pW0[0:1, 0:HTOK], c.invdcol[:, 0:1], Hx[:, t * HTOK:(t + 1) * HTOK],
            start=(t == 0), stop=(t == NT - 1)),
            reads=[f"Hx{t}", c.invdcol.name], writes=["pW0"])
        S.op("tensor", lambda h, t=t, sq=sq: h.matmul(
            c.pW1[0:1, 0:HTOK], c.invdcol[:, 0:1], sq[:, 0:HTOK],
            start=(t == 0), stop=(t == NT - 1)),
            reads=[sq.name, c.invdcol.name], writes=["pW1"])
    S.op("vector", lambda h: h.tensor_copy(c.r0[:1, 0:HTOK], c.pW0[0:1, 0:HTOK]),
         reads=["pW0"], writes=[c.r0.name])
    S.op("vector", lambda h: h.tensor_mul(c.r2[:1, 0:HTOK], c.r0[:1, 0:HTOK], c.r0[:1, 0:HTOK]),
         reads=[c.r0.name], writes=[c.r2.name])
    S.op("vector", lambda h: h.tensor_sub(c.r1[:1, 0:HTOK], c.pW1[0:1, 0:HTOK], c.r2[:1, 0:HTOK]),
         reads=["pW1", c.r2.name], writes=[c.r1.name])
    S.op("scalar", lambda h: h.activation(c.r2[:1, 0:HTOK], c.r1[:1, 0:HTOK], AF.Ln,
                                          bias=c.epscol[0:1, 0:1]),
         reads=[c.r1.name, c.epscol.name], writes=[c.r2.name])
    S.op("scalar", lambda h: h.activation(c.r1[:1, 0:HTOK], c.r2[:1, 0:HTOK], AF.Exp, scale=-0.5),
         reads=[c.r2.name], writes=[c.r1.name])
    S.op("tensor", lambda h: h.matmul(c.pW0[:, 0:HTOK], c.onesf[0:1, :],
                                      c.r0[0:1, 0:HTOK], start=True, stop=True),
         reads=[c.r0.name, c.onesf.name], writes=["pW0"])
    S.op("tensor", lambda h: h.matmul(c.pW1[:, 0:HTOK], c.onesf[0:1, :],
                                      c.r1[0:1, 0:HTOK], start=True, stop=True),
         reads=[c.r1.name, c.onesf.name], writes=["pW1"])
    for t in range(NT):
        tb = c.t32[t % 2]
        S.op("vector", lambda h, t=t, tb=tb: h.tensor_sub(
            tb[:, 0:HTOK], Hx[:, t * HTOK:(t + 1) * HTOK], c.pW0[:, 0:HTOK]),
            reads=[f"Hx{t}", "pW0"], writes=[tb.name])
        S.op("vector", lambda h, t=t, tb=tb: h.tensor_mul(
            Hx[:, t * HTOK:(t + 1) * HTOK], tb[:, 0:HTOK], c.pW1[:, 0:HTOK]),
            reads=[tb.name, "pW1"], writes=[f"Hx{t}"])
    cosk = S.sbuf("sb_cosk", [128, KEYS], BF16)
    sink = S.sbuf("sb_sink", [128, KEYS], BF16)
    S.dma("gpsimd", cosk[0:64, :], IN["cos_k"][:, :], writes=[cosk.name])
    S.dma("gpsimd", cosk[64:128, :], IN["cos_k"][:, :], writes=[cosk.name])
    S.dma("gpsimd", sink[0:64, :], IN["sin_k"][:, :], writes=[sink.name])
    S.dma("gpsimd", sink[64:128, :], IN["sin_k"][:, :], writes=[sink.name])
    csq = S.sbuf("csq", [128, TOK], BF16)
    ssq2 = S.sbuf("ssq2", [128, TOK], BF16)
    SCL = 1.0 / np.sqrt(128.0)
    S.op("vector", lambda h: h.tensor_scalar(csq[:, :], cosk[:, HTOK:], SCL, None, ALU.mult),
         reads=[cosk.name], writes=[csq.name])
    S.op("vector", lambda h: h.tensor_scalar(ssq2[:, :], sink[:, HTOK:], SCL, None, ALU.mult),
         reads=[sink.name], writes=[ssq2.name])
    tri_lo = S.sbuf("sb_tri_lo", [128, 128], BF16)
    tri_hi = S.sbuf("sb_tri_hi", [128, 128], BF16)
    S.dma("gpsimd", tri_lo[:, :], IN["tri_lo"][:, :], writes=[tri_lo.name])
    S.dma("gpsimd", tri_hi[:, :], IN["tri_hi"][:, :], writes=[tri_hi.name])
    vrow = S.sbuf("vrow", [1, HTOK], F32)
    S.dma("gpsimd", vrow[:, :], IN["valadd"][:, :], writes=[vrow.name])
    S.op("tensor", lambda h: h.matmul(c.pC[:, 0:HTOK], c.onesf[0:1, :], vrow[0:1, :],
                                      start=True, stop=True),
         reads=[vrow.name, c.onesf.name], writes=["pC"])
    valb = S.sbuf("valb", [128, HTOK], BF16)
    S.op("scalar", lambda h: h.copy(valb[:, :], c.pC[:, 0:HTOK]),
         reads=["pC"], writes=[valb.name])

    def rope(dst_ap, dst_key, src, src_key, ctab, stab, cs_keys, off, W):
        ta = c.sqb[0]
        tb = c.sqb[1]
        S.op("vector", lambda h: h.tensor_mul(ta[0:64, 0:W], src[0:64, 0:W],
                                              ctab[0:64, off:off + W]),
             reads=[src_key] + cs_keys, writes=["ropeA"])
        S.op("vector", lambda h: h.tensor_mul(tb[0:64, 0:W], src[64:128, 0:W],
                                              stab[64:128, off:off + W]),
             reads=[src_key] + cs_keys, writes=["ropeB"])
        S.op("vector", lambda h: h.tensor_sub(dst_ap[0:64, 0:W], ta[0:64, 0:W], tb[0:64, 0:W]),
             reads=["ropeA", "ropeB"], writes=[dst_key])
        S.op("vector", lambda h: h.tensor_mul(ta[0:64, 0:W], src[0:64, 0:W],
                                              stab[0:64, off:off + W]),
             reads=[src_key] + cs_keys, writes=["ropeA"])
        S.op("vector", lambda h: h.tensor_mul(tb[0:64, 0:W], src[64:128, 0:W],
                                              ctab[64:128, off:off + W]),
             reads=[src_key] + cs_keys, writes=["ropeB"])
        S.op("vector", lambda h: h.tensor_add(dst_ap[64:128, 0:W], ta[0:64, 0:W], tb[0:64, 0:W]),
             reads=["ropeA", "ropeB"], writes=[dst_key])

    k_sb = S.sbuf("k_sb", [128, KVH * KEYS], BF16)
    v_sb = S.sbuf("v_sb", [128, (KEYS // 128) * 512], BF16)
    kt32 = c.t32[0]
    for ci, wb, wk, o0, ow in stream_weight_chunks(S, c, "wkT5", IN["wkT5"], D, 512, 256):
        for ot2 in range(ow // 128):
            g = (o0 + ot2 * 128) // 128
            for kc in range(KEYS // 512):
                p, pk = (c.pA, "pA") if kc % 2 == 0 else (c.pB, "pB")
                for i in range(NT):
                    def mmk(h, p=p, wb=wb, i=i, ot2=ot2, kc=kc, ow=ow):
                        if kc == 0:
                            rhs = Hx[:, i * HTOK:(i + 1) * HTOK]
                        else:
                            rhs = c.A[:, i * TOK + (kc - 1) * 512: i * TOK + kc * 512]
                        return h.matmul(p[:, 0:512],
                                        wb[:, i * ow + ot2 * 128: i * ow + (ot2 + 1) * 128],
                                        rhs, start=(i == 0), stop=(i == NT - 1))
                    rk = f"Hx{i}" if kc == 0 else f"A{i}"
                    S.op("tensor", mmk, reads=[wk, rk], writes=[pk])
                S.op("scalar", lambda h, p=p: h.copy(kt32[:, 0:512], p[:, 0:512]),
                     reads=[pk], writes=[kt32.name])
                S.op("vector", lambda h: h.engine_nop(), reads=[], writes=[]) if False else None
                rope(k_sb[:, g * KEYS + kc * 512: g * KEYS + (kc + 1) * 512], f"k_{g}",
                     kt32, kt32.name, cosk, sink,
                     [cosk.name, sink.name], kc * 512, 512)
    for ci, wb, wk, o0, ow in stream_weight_chunks(S, c, "wvT5", IN["wvT5"], D, 512, 256):
        for tt in range(KEYS // 128):
            p, pk = (c.pA, "pA") if tt % 2 == 0 else (c.pB, "pB")
            for i in range(NT):
                def mmv(h, p=p, wb=wb, i=i, tt=tt, ow=ow, o0=o0):
                    if tt < 4:
                        lhs = Hx[:, i * HTOK + tt * 128: i * HTOK + (tt + 1) * 128]
                    else:
                        lhs = c.A[:, i * TOK + (tt - 4) * 128: i * TOK + (tt - 3) * 128]
                    return h.matmul(p[:, 0:ow], lhs, wb[:, i * ow:(i + 1) * ow],
                                    start=(i == 0), stop=(i == NT - 1))
                rk = f"Hx{i}" if tt < 4 else f"A{i}"
                S.op("tensor", mmv, reads=[wk, rk], writes=[pk])
            S.op("scalar", lambda h, p=p, tt=tt, o0=o0, ow=ow: h.copy(
                v_sb[:, tt * 512 + o0: tt * 512 + o0 + ow], p[:, 0:ow]),
                reads=[pk], writes=[f"v_{tt}"])
    apb = S.sbuf("apb", [128, 640], BF16)
    esb = apb
    pT5 = S.sbuf("pT5", [128, 640], BF16)
    ncol = S.sbuf("ncol", [128, 2], F32)
    for ci, wb, wk, o0, ow in stream_weight_chunks(S, c, "wqT5", IN["wqT5"], D, D, 256):
        for ot2 in range(ow // 128):
            hq = (o0 + ot2 * 128) // 128
            g = hq // 4
            for ch in range(NCh):
                p, pk = (c.pA, "pA") if ch % 2 == 0 else (c.pB, "pB")
                for i in range(NT):
                    S.op("tensor", lambda h, p=p, wb=wb, i=i, ot2=ot2, ch=ch, ow=ow: h.matmul(
                        p[:, 0:CW], wb[:, i * ow + ot2 * 128: i * ow + (ot2 + 1) * 128],
                        c.A[:, i * TOK + ch * CW: i * TOK + (ch + 1) * CW],
                        start=(i == 0), stop=(i == NT - 1)),
                        reads=[wk, f"A{i}"], writes=[pk])
                S.op("scalar", lambda h, p=p: h.copy(kt32[:, 0:CW], p[:, 0:CW]),
                     reads=[pk], writes=[kt32.name])
                rope(c.B[:, hq * TOK + ch * CW: hq * TOK + (ch + 1) * CW], f"B{hq}",
                     kt32, kt32.name, csq, ssq2,
                     [csq.name, ssq2.name], ch * CW, CW)
            for qt in range(nqt):
                ps, psk = (c.pW0, "pW0") if qt % 2 == 0 else (c.pW1, "pW1")
                for kt in range(5):
                    S.op("tensor", lambda h, ps=ps, hq=hq, g=g, qt=qt, kt=kt: h.matmul(
                        ps[:, kt * 128:(kt + 1) * 128],
                        c.B[:, hq * TOK + qt * 128: hq * TOK + (qt + 1) * 128],
                        k_sb[:, g * KEYS + qt * 128 + kt * 128: g * KEYS + qt * 128 + (kt + 1) * 128],
                        start=True, stop=True),
                        reads=[f"B{hq}", f"k_{g}"], writes=[psk])
                S.op("vector", lambda h, ps=ps: h.tensor_add(ps[:, 512:640], ps[:, 512:640],
                                                             tri_lo[:, :]),
                     reads=[psk, tri_lo.name], writes=[psk])
                S.op("vector", lambda h, ps=ps: h.tensor_add(ps[:, 0:128], ps[:, 0:128],
                                                             tri_hi[:, :]),
                     reads=[psk, tri_hi.name], writes=[psk])
                if qt <= 3:
                    S.op("vector", lambda h, ps=ps, qt=qt: h.tensor_add(
                        ps[:, 0:HTOK - qt * 128], ps[:, 0:HTOK - qt * 128],
                        valb[:, qt * 128:HTOK]),
                        reads=[psk, valb.name], writes=[psk])
                S.op("vector", lambda h, ps=ps: h.reduce_max(ncol[:, 0:1], ps[:, 0:640],
                                                             axis=AX.X, negate=True),
                     reads=[psk], writes=[ncol.name])
                S.op("scalar", lambda h, ps=ps: h.activation(apb[:, :], ps[:, 0:640], AF.Exp,
                                                             bias=ncol[:, 0:1],
                                                             accum_out=ncol[:, 1:2]),
                     reads=[psk, ncol.name], writes=[apb.name, "nsum"])
                S.op("vector", lambda h: h.reciprocal(ncol[:, 1:2], ncol[:, 1:2]),
                     reads=["nsum"], writes=["nrec"])
                S.op("vector", lambda h: h.tensor_scalar(apb[:, :], apb[:, :], ncol[:, 1:2],
                                                         None, ALU.mult),
                     reads=[apb.name, "nrec"], writes=[apb.name])
                for kt in range(5):
                    S.op("tensor", lambda h, kt=kt: h.transpose(
                        c.pBF[:, kt * 128:(kt + 1) * 128], apb[:, kt * 128:(kt + 1) * 128],
                        c.idnb[:, :]),
                        reads=[apb.name, c.idnb.name], writes=["pBF"])
                S.op("scalar", lambda h: h.copy(pT5[:, 0:640], c.pBF[:, 0:640]),
                     reads=["pBF"], writes=[pT5.name])
                for kt in range(5):
                    S.op("tensor", lambda h, g=g, qt=qt, kt=kt: h.matmul(
                        c.pC[:, 0:128], v_sb[:, (qt + kt) * 512 + g * 128:(qt + kt) * 512 + (g + 1) * 128],
                        pT5[:, kt * 128:(kt + 1) * 128],
                        start=(kt == 0), stop=(kt == 4)),
                        reads=[f"v_{qt + kt}", pT5.name], writes=["pC"])
                S.op("scalar", lambda h, hq=hq, qt=qt: h.copy(
                    c.B[:, hq * TOK + qt * 128: hq * TOK + (qt + 1) * 128], c.pC[:, 0:128]),
                    reads=["pC"], writes=[f"B{hq}"])
    for ci, wb, wk, o0, ow in stream_weight_chunks(S, c, "woT5", IN["woT5"], D, D, 256):
        for ot2 in range(ow // 128):
            o = (o0 + ot2 * 128) // 128
            for ch in range(NCh):
                p, pk = (c.pA, "pA") if (o * NCh + ch) % 2 == 0 else (c.pB, "pB")
                for i in range(NT):
                    S.op("tensor", lambda h, p=p, wb=wb, i=i, ot2=ot2, ch=ch, ow=ow: h.matmul(
                        p[:, 0:CW], wb[:, i * ow + ot2 * 128: i * ow + (ot2 + 1) * 128],
                        c.B[:, i * TOK + ch * CW: i * TOK + (ch + 1) * CW],
                        start=(i == 0), stop=(i == NT - 1)),
                        reads=[wk, f"B{i}"], writes=[pk])
                S.op("vector", lambda h, p=p, o=o, ch=ch: h.tensor_add(
                    c.x[:, o * TOK + ch * CW: o * TOK + (ch + 1) * CW],
                    c.x[:, o * TOK + ch * CW: o * TOK + (ch + 1) * CW], p[:, 0:CW]),
                    reads=[pk, f"x{o}"], writes=[f"x{o}"])



CONSTS = [("idn_f32", [128, 128], F32), ("idn_bf", [128, 128], BF16),
          ("ones_bf", [1, 128], BF16), ("ones_f32", [1, 128], F32),
          ("invd_bf", [128, 2], BF16), ("eps_col", [128, 2], F32)]
MOE_W = [("gwT_h", [NT * 128, 8], F32), ("gb2", [8, 2], F32), ("b2", [8, D], F32),
         ("b1c", [NE * 128, 11], F32), ("w1T", [NE * D, EH], F32), ("w2T", [NE * EH, D], F32)]
L0A_W = [("wqT", [D, D], F32), ("woT", [D, D], F32), ("mnT_h", [16 * 128, 4], F32),
         ("mn_nat", [4, 2048], F32), ("bank_w", [16, 4], F32),
         ("mem_h", [4 * 16 * 128, 128], F32)]
FFNPRE_W = [("cwT_h", [NT * 128, 16], F32), ("npw_cols", [128, 16], F32),
            ("cb", [16, 1], F32), ("wwT", [D, D], F32)]
FFNFIN_W = [("npost_cols", [128, 16], F32), ("bq_cols", [128, 16], F32),
            ("bk_cols", [128, 16], F32), ("opb_cols", [128, 16], F32),
            ("npost_row", [1, D], F32), ("bv_row", [1, D], F32), ("wb_row", [1, D], F32),
            ("wkT1", [D, D], F32), ("wvT1", [D, D], F32), ("wqT1", [D, D], F32),
            ("opT", [D, D], F32)]
L1A_W = [("wqT5", [D, D], F32), ("wkT5", [D, 512], F32), ("wvT5", [D, 512], F32),
         ("woT5", [D, D], F32), ("tri_lo", [128, 128], F32), ("tri_hi", [128, 128], F32)]


def build_launch(kind, TOK, CPB):
    nc = bass.Bass("TRN2", target_bir_lowering=False, num_devices=8)
    IN = {}

    def dram(nm, sh, dt, kindd="ExternalInput"):
        IN[nm] = nc.dram_tensor(nm, sh, dt, kind=kindd)

    for nm, sh, dt in CONSTS:
        dram(nm, sh, dt)
    if kind == "c":
        dram("xin", [D, TOK], F32)
        for nm, sh, dt in MOE_W + L0A_W + FFNPRE_W:
            dram(nm, sh, dt)
        dram("bounce", [64, 4], F32, "Internal")
        dram("xout", [D, TOK], F32, "ExternalOutput")
        dram("innerout", [D, TOK], F32, "ExternalOutput")
        dram("pay_out", [16, 4 + 4 * D], F32, "ExternalOutput")
    elif kind == "a":
        dram("innerin", [D, TOK], F32)
        dram("wsin", [16, D], F32)
        dram("pay_in", [16, CPB * (4 + 4 * D)], F32)
        for nm, sh, dt in FFNPRE_W + FFNFIN_W:
            dram(nm, sh, dt)
        dram("innerout", [D, TOK], F32, "ExternalOutput")
        dram("wsout", [16, D], F32, "ExternalOutput")
        dram("pay_out", [16, 4 + 4 * D], F32, "ExternalOutput")
    elif kind == "b":
        dram("innerin", [D, TOK], F32)
        dram("xoin", [D, TOK], F32)
        dram("wsin", [16, D], F32)
        dram("pay_in", [16, CPB * (4 + 4 * D)], F32)
        for nm, sh, dt in FFNFIN_W:
            dram(nm, sh, dt)
        dram("cwT_h", [NT * 128, 16], F32)
        dram("npw_cols", [128, 16], F32)
        dram("cb", [16, 1], F32)
        dram("xout", [D, TOK], F32, "ExternalOutput")
    elif kind == "d":
        dram("xin", [D, TOK], F32)
        dram("xhalo", [D, 512], F32)
        dram("cos_k", [64, 512 + TOK], F32)
        dram("sin_k", [64, 512 + TOK], F32)
        dram("valadd", [1, 512], F32)
        for nm, sh, dt in L1A_W + FFNPRE_W:
            dram(nm, sh, dt)
        dram("xout", [D, TOK], F32, "ExternalOutput")
        dram("innerout", [D, TOK], F32, "ExternalOutput")
        dram("pay_out", [16, 4 + 4 * D], F32, "ExternalOutput")

    S = Sched(nc)
    gcols = 11 * TOK if kind == "c" else 0
    if kind == "c" and gcols < 64 * 128 + 2 * TOK:
        gcols = 64 * 128 + 2 * TOK
    c = alloc_common(S, TOK, gcols=gcols, need_r0=(kind in ("c", "d")))
    load_consts(S, c, IN)
    fin_keys = []
    if kind == "c":
        load_x(S, c, IN["xin"])
        emit_moe(S, c, IN, 0)
        emit_l0attn(S, c, IN)
        emit_moe(S, c, IN, 1)
        fin_keys += store_x(S, c, IN["xout"], tag="xout")
        emit_layernorm(S, c, c.A, eps_idx=0, also_x=True, dstname="A")
        fin_keys += store_x(S, c, IN["innerout"], tag="innerout")
        P = emit_l1ffn_prologue(S, c, IN, want_fin=False)
        emit_l1ffn_pre(S, c, IN, P)
        fin_keys += ["pay_out"]
    elif kind == "a":
        load_x(S, c, IN["innerin"])
        P = emit_l1ffn_prologue(S, c, IN, want_fin=True)
        emit_l1ffn_finish(S, c, IN, P, CPB, True)
        fin_keys += store_x(S, c, IN["innerout"], tag="innerout")
        S.dma("gpsimd", IN["wsout"][:, :], S._tensors["wsin_sb"][:, :],
              reads=["wss"], writes=["wsout"])
        fin_keys += ["wsout"]
        emit_l1ffn_pre(S, c, IN, P)
        fin_keys += ["pay_out"]
    elif kind == "b":
        load_x(S, c, IN["innerin"])
        P = emit_l1ffn_prologue(S, c, IN, want_fin=True)
        emit_l1ffn_finish(S, c, IN, P, CPB, False)
        TOKc, CW, NCh = c.TOK, c.CW, c.NC
        for t in range(NT):
            for ch in range(NCh):
                tb = c.t32[(t * NCh + ch) % 2]
                S.dma("sync", tb[:, 0:CW],
                      IN["xoin"][t * 128:(t + 1) * 128, ch * CW:(ch + 1) * CW],
                      writes=[tb.name])
                S.op("vector", lambda h, t=t, ch=ch, tb=tb: h.scalar_tensor_tensor(
                    c.x[:, t * TOKc + ch * CW: t * TOKc + (ch + 1) * CW],
                    c.x[:, t * TOKc + ch * CW: t * TOKc + (ch + 1) * CW],
                    0.5, tb[:, 0:CW], ALU.mult, ALU.add),
                    reads=[f"x{t}", tb.name], writes=[f"x{t}"])
        fin_keys += store_x(S, c, IN["xout"], tag="xout")
    elif kind == "d":
        load_x(S, c, IN["xin"])
        emit_l1attn(S, c, IN)
        fin_keys += store_x(S, c, IN["xout"], tag="xout")
        emit_layernorm(S, c, c.A, eps_idx=0, also_x=True, dstname="A")
        fin_keys += store_x(S, c, IN["innerout"], tag="innerout")
        P = emit_l1ffn_prologue(S, c, IN, want_fin=False)
        emit_l1ffn_pre(S, c, IN, P)
        fin_keys += ["pay_out"]
    S.op("sync", lambda h: h.nop(), reads=fin_keys)
    S.finalize()
    return nc


def host_weights(ins):
    T = lambda a: np.ascontiguousarray(np.asarray(a).T.astype(np.float32))
    A = lambda a: np.ascontiguousarray(np.asarray(a).astype(np.float32))
    w = {}
    import ml_dtypes
    BFD = ml_dtypes.bfloat16
    w["idn_f32"] = np.eye(128, dtype=np.float32)
    w["idn_bf"] = np.eye(128, dtype=BFD)
    w["ones_bf"] = np.ones((1, 128), BFD)
    w["ones_f32"] = np.ones((1, 128), np.float32)
    w["invd_bf"] = np.full((128, 2), 1.0 / 2048, BFD)
    w["eps_col"] = np.tile(np.array([[1e-5, 1e-6]], np.float32), (128, 1))
    w["gwT_h"] = T(ins["L0_ffn_gate_w"])
    w["gb2"] = np.stack([A(ins["L0_ffn_gate_b"]), A(ins["L0_ffn_bias"])], 1)
    w["b2"] = A(ins["L0_ffn_b2"])
    w["b1c"] = A(ins["L0_ffn_b1"]).reshape(NE, 11, 128).transpose(0, 2, 1).reshape(NE * 128, 11)
    w["w1T"] = A(ins["L0_ffn_w1"]).transpose(0, 2, 1).reshape(NE * D, EH)
    w["w2T"] = A(ins["L0_ffn_w2"]).transpose(0, 2, 1).reshape(NE * EH, D)
    w["wqT"] = T(ins["L0_attn_Wq"])
    w["woT"] = T(ins["L0_attn_Wo"])
    w["mnT_h"] = A(ins["L0_attn_memory_norms"]).transpose(1, 2, 0).reshape(16 * 128, 4)
    w["mn_nat"] = A(ins["L0_attn_memory_norms"]).reshape(4, 2048)
    w["bank_w"] = A(ins["L0_attn_bank_weights"])
    w["mem_h"] = A(ins["L0_attn_memories"]).reshape(4 * 16 * 128, 128)
    w["cwT_h"] = T(ins["L1_ffn_compete_w"])
    w["npw_cols"] = A(ins["L1_ffn_norm_pre_w"]).reshape(16, 128).T.copy()
    w["cb"] = A(ins["L1_ffn_compete_b"]).reshape(16, 1)
    w["wwT"] = T(ins["L1_ffn_write_w"])
    ipw, ipb = np.asarray(ins["L1_ffn_in_proj_w"]), np.asarray(ins["L1_ffn_in_proj_b"])
    w["npost_cols"] = A(ins["L1_ffn_norm_post_w"]).reshape(16, 128).T.copy()
    w["bq_cols"] = A(ipb[:D]).reshape(16, 128).T.copy()
    w["bk_cols"] = A(ipb[D:2 * D]).reshape(16, 128).T.copy()
    w["opb_cols"] = A(ins["L1_ffn_out_proj_b"]).reshape(16, 128).T.copy()
    w["npost_row"] = A(ins["L1_ffn_norm_post_w"]).reshape(1, D)
    w["bv_row"] = A(ipb[2 * D:]).reshape(1, D)
    w["wb_row"] = A(ins["L1_ffn_write_b"]).reshape(1, D)
    w["wkT1"] = T(ipw[D:2 * D])
    w["wvT1"] = T(ipw[2 * D:])
    w["wqT1"] = T(ipw[:D])
    w["opT"] = T(ins["L1_ffn_out_proj_w"])
    w["wqT5"] = T(ins["L1_attn_wq"])
    w["wkT5"] = T(ins["L1_attn_wk"])
    w["wvT5"] = T(ins["L1_attn_wv"])
    w["woT5"] = T(ins["L1_attn_wo"])
    i = np.arange(128)
    w["tri_lo"] = np.where(i[None, :] <= i[:, None], 0.0, -1e30).astype(np.float32)
    w["tri_hi"] = np.where(i[None, :] >= i[:, None], 0.0, -1e30).astype(np.float32)
    w["wsin0"] = A(ins["L1_ffn_workspace"])
    return w


def rope_tables_for(pos0, n, scale_half=64):
    half = 64
    inv = 1.0 / (10000.0 ** (np.arange(half, dtype=np.float64) / half))
    t = np.arange(pos0, pos0 + n, dtype=np.float64)
    fr = np.outer(inv, t)
    return np.cos(fr).astype(np.float32), np.sin(fr).astype(np.float32)


TOKP = 1024
CPBP = 4
NCORES = 8
_prog_cache = {}


def _get_prog(kind):
    if kind not in _prog_cache:
        _prog_cache[kind] = build_launch(kind, TOKP, CPBP)
    return _prog_cache[kind]


def _run(kind, in_maps, outs):
    nc = _get_prog(kind)
    res = run_bass_kernel_spmd(nc, in_maps, core_ids=list(range(NCORES)))
    return [{nm: np.asarray(r[nm]) for nm in outs} for r in res.results]


def kernel(**inputs):
    W = host_weights(inputs)
    x0 = np.asarray(inputs["L0_attn_hidden_states"], dtype=np.float32)
    xs = []
    for cc in range(NCORES):
        b, k = cc // CPBP, cc % CPBP
        xs.append(np.ascontiguousarray(x0[b, k * TOKP:(k + 1) * TOKP, :].T))

    CONSTN = [n for n, _, _ in CONSTS]

    def wsel(names):
        return {k: W[k] for k in names if k in W}

    def gather_pays(rs):
        out = []
        for cc in range(NCORES):
            b = cc // CPBP
            out.append(np.ascontiguousarray(
                np.concatenate([rs[b * CPBP + j]["pay_out"] for j in range(CPBP)], axis=1)))
        return out

    base_c = wsel(CONSTN + [n for n, _, _ in MOE_W + L0A_W + FFNPRE_W])
    r1 = _run("c", [{**base_c, "xin": xs[cc]} for cc in range(NCORES)],
              ("xout", "innerout", "pay_out"))
    base_a = wsel(CONSTN + [n for n, _, _ in FFNPRE_W + FFNFIN_W])
    pays = gather_pays(r1)
    r2 = _run("a", [{**base_a, "innerin": r1[cc]["innerout"], "wsin": W["wsin0"],
                     "pay_in": pays[cc]} for cc in range(NCORES)],
              ("innerout", "wsout", "pay_out"))
    base_b = wsel(CONSTN + [n for n, _, _ in FFNFIN_W] + ["cwT_h", "npw_cols", "cb"])
    pays = gather_pays(r2)
    r3 = _run("b", [{**base_b, "innerin": r2[cc]["innerout"], "xoin": r1[cc]["xout"],
                     "wsin": r2[cc]["wsout"], "pay_in": pays[cc]} for cc in range(NCORES)],
              ("xout",))
    base_d = wsel(CONSTN + [n for n, _, _ in L1A_W + FFNPRE_W])
    feeds = []
    zhalo = np.zeros((D, 512), np.float32)
    vneg = np.full((1, 512), -1e30, np.float32)
    vzero = np.zeros((1, 512), np.float32)
    for cc in range(NCORES):
        b, k = cc // CPBP, cc % CPBP
        cosk, sink = rope_tables_for(k * TOKP - 512, 512 + TOKP)
        if k == 0:
            xhalo, val = zhalo, vneg
        else:
            xhalo = np.ascontiguousarray(r3[cc - 1]["xout"][:, TOKP - 512:])
            val = vzero
        feeds.append({**base_d, "xin": r3[cc]["xout"], "xhalo": xhalo,
                      "cos_k": cosk, "sin_k": sink, "valadd": val})
    r4 = _run("d", feeds, ("xout", "innerout", "pay_out"))
    pays = gather_pays(r4)
    r5 = _run("a", [{**base_a, "innerin": r4[cc]["innerout"], "wsin": W["wsin0"],
                     "pay_in": pays[cc]} for cc in range(NCORES)],
              ("innerout", "wsout", "pay_out"))
    pays = gather_pays(r5)
    r6 = _run("b", [{**base_b, "innerin": r5[cc]["innerout"], "xoin": r4[cc]["xout"],
                     "wsin": r5[cc]["wsout"], "pay_in": pays[cc]} for cc in range(NCORES)],
              ("xout",))
    out = np.stack([np.concatenate([r6[b * CPBP + j]["xout"].T for j in range(CPBP)], 0)
                    for b in range(2)]).astype(np.float32)
    return out
